# version 01 (final)
# baseline (speedup 1.0000x reference)
import contextlib
import numpy as np
import ml_dtypes
import concourse.bass as bass
import concourse.mybir as mybir

F32 = mybir.dt.float32
BF16 = mybir.dt.bfloat16
U8 = mybir.dt.uint8
AF = mybir.ActivationFunctionType
ALU = mybir.AluOpType
AX = mybir.AxisListType.X

T = 2048
D = 2048
H = 8192
NH, HD, BLK = 16, 128, 256
EPS_LN = 1e-5
EPS_RMS = 1e-6

C_INV2048, C_EPSLN, C_EPSRMS, C_NEGHALF, C_HALF, C_ONE, C_NEG1, C_256, C_1P5, C_ZERO = range(10)
C_CNT0 = 10
C_ISQT = 26
C_ISQH = 27
NC = 28
CONST_VALS = {C_INV2048: 1.0 / 2048, C_EPSLN: EPS_LN, C_EPSRMS: EPS_RMS,
              C_NEGHALF: -0.5, C_HALF: 0.5, C_ONE: 1.0, C_NEG1: -1.0,
              C_256: 256.0, C_1P5: 1.5, C_ZERO: 0.0, C_ISQT: 1.0 / float(np.sqrt(2048.0)),
              C_ISQH: 1.0 / float(np.sqrt(8192.0))}
for _qi in range(16):
    CONST_VALS[C_CNT0 + _qi] = float(2048 - (_qi + 1) * 128)


class SemC:
    __slots__ = ("h", "v")

    def __init__(self, h):
        self.h = h
        self.v = 0


class Prog:
    def __init__(self):
        self.nc = bass.Bass("TRN2", target_bir_lowering=False,
                            detect_race_conditions=False)
        self.es = contextlib.ExitStack()
        self.uid = 0
        self.d = {}
        self.s = {}
        self.semcs = {}

    def name(self, base):
        self.uid += 1
        return f"{base}_{self.uid}"

    def dram(self, name, shape, dtype, kind="Internal"):
        t = self.nc.dram_tensor(name, list(shape), dtype, kind=kind)
        self.d[name] = t.ap()
        return self.d[name]

    def sbuf(self, es, base, shape, dtype):
        return es.enter_context(self.nc.sbuf_tensor(self.name(base), list(shape), dtype))

    def psum(self, es, base, shape, dtype=F32):
        return es.enter_context(self.nc.psum_tensor(self.name(base), list(shape), dtype))

    def sem(self, name):
        if name not in self.semcs:
            h = self.es.enter_context(self.nc.semaphore(name))
            self.semcs[name] = SemC(h)
        return self.semcs[name]


def host_prep(inputs, core):
    f32 = np.float32
    bf = ml_dtypes.bfloat16
    ins = {}
    ins["xin"] = np.ascontiguousarray(inputs["L0_attn_hidden_states"][core]).astype(f32, copy=False)
    ins["w1t"] = np.ascontiguousarray(inputs["L0_ffn_W1"].T)
    ins["w3t"] = np.ascontiguousarray(inputs["L0_ffn_W3"].T)
    ins["w2t"] = np.ascontiguousarray(inputs["L0_ffn_W2"].T)
    ins["wqkvt"] = np.ascontiguousarray(inputs["L0_attn_w_qkv"].T)
    ins["wg0t"] = np.ascontiguousarray(inputs["L0_attn_w_gate"].T)
    ins["wo0t"] = np.ascontiguousarray(inputs["L0_attn_w_out"].T)
    ins["normw"] = np.ascontiguousarray(inputs["L0_attn_norm_w"].reshape(2048, 1))
    ins["slope"] = np.ascontiguousarray(np.broadcast_to(inputs["L0_attn_slope_rate"].reshape(1, 16), (128, 16)))
    ins["wq1"] = np.ascontiguousarray(inputs["L1_attn_Wq"])
    ins["wk1"] = np.ascontiguousarray(inputs["L1_attn_Wk"])
    ins["wv1"] = np.ascontiguousarray(inputs["L1_attn_Wv"])
    ins["wo1"] = np.ascontiguousarray(inputs["L1_attn_Wo"])
    ins["wg1"] = np.ascontiguousarray(inputs["L1_ffn_w_gate"])
    ins["wu1"] = np.ascontiguousarray(inputs["L1_ffn_w_up"])
    ins["wd1"] = np.ascontiguousarray(inputs["L1_ffn_w_down"])
    ins["ident"] = np.eye(128, dtype=bf)
    hmat = np.array([[1.0]], dtype=np.float64)
    while hmat.shape[0] < 128:
        hmat = np.block([[hmat, hmat], [hmat, -hmat]])
    ins["h128"] = hmat.astype(bf)
    ins["h64"] = hmat[:64, :64].astype(bf)
    ins["tril"] = np.tril(np.ones((128, 128), dtype=f32))
    m = np.arange(256, dtype=f32)
    p2 = np.arange(128, dtype=f32)
    ins["arr256"] = np.broadcast_to((m + 1.0)[None, :], (128, 256)).copy()
    idxc = np.zeros((128, 2, 256), dtype=f32)
    maskd = np.zeros((128, 2, 256), dtype=f32)
    for j in range(2):
        n = j * 128 + p2[:, None]
        idxc[:, j, :] = np.maximum(m[None, :] - n, 0.0)
        maskd[:, j, :] = (m[None, :] >= n).astype(f32)
    ins["idxc"] = idxc
    ins["maskd"] = maskd
    rev = np.zeros((128, 2), dtype=f32)
    for j in range(2):
        rev[:, j] = 255.0 - (j * 128 + p2)
    ins["rev"] = rev
    cstv = np.zeros((128, NC), dtype=f32)
    for idx, val in CONST_VALS.items():
        cstv[:, idx] = val
    ins["cstin"] = cstv
    ins["onesin"] = np.ones((128, 128), dtype=bf)
    return ins


ALL_EXT = ["xin", "w1t", "w3t", "w2t", "wqkvt", "wg0t", "wo0t", "normw", "slope",
           "wq1", "wk1", "wv1", "wo1", "wg1", "wu1", "wd1",
           "ident", "h128", "h64", "tril", "arr256", "idxc", "maskd", "rev",
           "cstin", "onesin"]


def declare_io(p, need=None):
    def ext(name, shape, dtype=F32):
        if need is None or name in need:
            p.dram(name, shape, dtype, kind="ExternalInput")
    ext("xin", [T, D])
    ext("w1t", [D, H]); ext("w3t", [D, H]); ext("w2t", [H, D])
    ext("wqkvt", [D, 3 * D]); ext("wg0t", [D, D]); ext("wo0t", [D, D])
    ext("normw", [D, 1]); ext("slope", [128, 16])
    ext("wq1", [D, D]); ext("wk1", [D, D]); ext("wv1", [D, D]); ext("wo1", [D, D])
    ext("wg1", [D, H]); ext("wu1", [D, H]); ext("wd1", [H, D])
    ext("ident", [128, 128], BF16); ext("h128", [128, 128], BF16); ext("h64", [64, 64], BF16)
    ext("tril", [128, 128], F32); ext("arr256", [128, 256], F32)
    ext("idxc", [128, 2, 256], F32); ext("maskd", [128, 2, 256], F32); ext("rev", [128, 2], F32)
    ext("cstin", [128, NC], F32); ext("onesin", [128, 128], BF16)
    p.dram("yout", [T, D], F32, kind="ExternalOutput")


SCRATCH_SHAPES = {
    "xa": ([T, D], F32), "xb": ([T, D], F32),
    "w2p": ([H, D], BF16), "utmp": ([H, D], BF16),
    "w1b": ([D, H], BF16), "w3b": ([D, H], BF16),
    "wqkvb": ([D, 3 * D], BF16), "wg0b": ([D, D], BF16), "wo0b": ([D, D], BF16),
    "wq1b": ([D, D], BF16), "wk1b": ([D, D], BF16), "wv1b": ([D, D], BF16), "wo1b": ([D, D], BF16),
    "wg1b": ([D, H], BF16), "wu1b": ([D, H], BF16), "wd1b": ([H, D], BF16),
    "qtsp": ([NH, HD, T], BF16),
    "gsp": ([16, 128, 16, 128], BF16),
}


def declare_scratch(p, names):
    for n in names:
        sh, dt = SCRATCH_SHAPES[n]
        p.dram(n, sh, dt)


def emit_consts(p):
    nc = p.nc
    es = p.es
    p.s["cst"] = p.sbuf(es, "cst", [128, NC], F32)
    p.s["ident"] = p.sbuf(es, "ident", [128, 128], BF16)
    p.s["ones_bf"] = p.sbuf(es, "ones_bf", [128, 128], BF16)
    cst, ident, ones_bf = p.s["cst"], p.s["ident"], p.s["ones_bf"]
    ld = p.sem("cst_ld")
    p.s["warm"] = p.sbuf(es, "warm", [128, 1], F32)
    warm = p.s["warm"]
    with nc.Block() as b:
        @b.sync
        def _(sync):
            sync.dma_start(out=ident[:], in_=p.d["ident"][:, :]).then_inc(ld.h, 16)
            sync.dma_start(out=cst[:], in_=p.d["cstin"][:, :]).then_inc(ld.h, 16)
            sync.dma_start(out=ones_bf[:], in_=p.d["onesin"][:, :]).then_inc(ld.h, 16)
            sync.wait_ge(ld.h, ld.v + 48)
        @b.scalar
        def _(s):
            s.wait_ge(ld.h, ld.v + 48)
            for f in (AF.Exp, AF.Sigmoid, AF.Silu, AF.Sqrt, AF.Square):
                s.activation(out=warm[:], in_=cst[:, C_ZERO:C_ZERO + 1], func=f)
    ld.v += 48


def cc(p, idx):
    return p.s["cst"][:, idx:idx + 1]


def emit_cast_weights(p, jobs):
    nc = p.nc
    cs = p.sem("cast_st")
    n = 0
    with nc.Block() as b:
        @b.gpsimd
        def _(g):
            nonlocal n
            for src, dst, rows, cols in jobs:
                chunk = max(1, min(rows, (1 << 22) // cols))
                r = 0
                while r < rows:
                    r2 = min(rows, r + chunk)
                    g.dma_start(out=p.d[dst][r:r2, :], in_=p.d[src][r:r2, :]).then_inc(cs.h, 16)
                    n += 1
                    r = r2
            g.wait_ge(cs.h, cs.v + 16 * n)
    cs.v += 16 * n


def emit_fold_norm_wo0(p):
    nc = p.nc
    with contextlib.ExitStack() as es:
        wsb = [p.sbuf(es, "fw", [128, D], F32) for _ in range(2)]
        nwsb = [p.sbuf(es, "fn", [128, 1], F32) for _ in range(2)]
        osb = [p.sbuf(es, "fo", [128, D], BF16) for _ in range(2)]
        ld = [p.sem("fold_ld0"), p.sem("fold_ld1")]
        st = [p.sem("fold_st0"), p.sem("fold_st1")]
        cm = p.sem("fold_cm")
        b_ld = [ld[0].v, ld[1].v]; b_st = [st[0].v, st[1].v]; b_cm = cm.v
        with nc.Block() as b:
            @b.sync
            def _(sync):
                for c in range(16):
                    pr = c % 2
                    if c >= 2:
                        sync.wait_ge(cm.h, b_cm + c - 1)
                    sync.dma_start(out=wsb[pr][:], in_=p.d["wo0t"][c * 128:(c + 1) * 128, :]).then_inc(ld[pr].h, 16)
                    sync.dma_start(out=nwsb[pr][:], in_=p.d["normw"][c * 128:(c + 1) * 128, :]).then_inc(ld[pr].h, 16)
            @b.vector
            def _(v):
                for c in range(16):
                    pr = c % 2
                    v.wait_ge(ld[pr].h, b_ld[pr] + 32 * (c // 2 + 1))
                    if c >= 2:
                        v.wait_ge(st[pr].h, b_st[pr] + 16 * (c // 2))
                    v.tensor_scalar(osb[pr][:], wsb[pr][:], nwsb[pr][:, :], None, ALU.mult).then_inc(cm.h, 1)
            @b.scalar
            def _(s):
                for c in range(16):
                    pr = c % 2
                    s.wait_ge(cm.h, b_cm + c + 1)
                    s.dma_start(out=p.d["wo0b"][c * 128:(c + 1) * 128, :], in_=osb[pr][:]).then_inc(st[pr].h, 16)
                s.wait_ge(st[0].h, b_st[0] + 16 * 8)
                s.wait_ge(st[1].h, b_st[1] + 16 * 8)
        ld[0].v += 32 * 8; ld[1].v += 32 * 8
        cm.v += 16
        st[0].v += 16 * 8; st[1].v += 16 * 8


def emit_w2prep(p):
    nc = p.nc
    with contextlib.ExitStack() as es:
        h64s = p.sbuf(es, "h64s", [64, 64], BF16)
        h128s = p.sbuf(es, "h128s", [128, 128], BF16)
        wblk = [p.sbuf(es, "wblk", [64, D], F32) for _ in range(2)]
        wbbf = [p.sbuf(es, "wbbf", [64, D], BF16) for _ in range(2)]
        ublk = [p.sbuf(es, "ublk", [128, D], BF16) for _ in range(2)]
        vout = [p.sbuf(es, "vout", [128, D], BF16) for _ in range(2)]
        ps = [p.psum(es, "w2ps", [128, D], F32) for _ in range(2)]
        ldh = p.sem("w2_ldh")
        ld = [p.sem("w2_ld0"), p.sem("w2_ld1")]
        st = [p.sem("w2_st0"), p.sem("w2_st1")]
        cv = p.sem("w2_cv"); mm = p.sem("w2_mm"); ev = p.sem("w2_ev")
        b_ldh = ldh.v; b_ld = [ld[0].v, ld[1].v]; b_st = [st[0].v, st[1].v]
        b_cv, b_mm, b_ev = cv.v, mm.v, ev.v
        with nc.Block() as b:
            @b.sync
            def _(sync):
                sync.dma_start(out=h64s[:], in_=p.d["h64"][:, :]).then_inc(ldh.h, 16)
                sync.dma_start(out=h128s[:], in_=p.d["h128"][:, :]).then_inc(ldh.h, 16)
                for a2 in range(128):
                    pr = a2 % 2
                    if a2 >= 2:
                        sync.wait_ge(cv.h, b_cv + a2 - 1)
                    sync.dma_start(out=wblk[pr][:], in_=p.d["w2t"][a2 * 64:(a2 + 1) * 64, :]).then_inc(ld[pr].h, 16)
            @b.vector
            def _(v):
                v.wait_ge(ldh.h, b_ldh + 32)
                for a2 in range(128):
                    pr = a2 % 2
                    v.wait_ge(ld[pr].h, b_ld[pr] + 16 * (a2 // 2 + 1))
                    if a2 >= 2:
                        v.wait_ge(mm.h, b_mm + 4 * (a2 - 1))
                    v.tensor_copy(wbbf[pr][:], wblk[pr][:]).then_inc(cv.h, 1)
                    if a2 >= 1:
                        epr = (a2 - 1) % 2
                        v.wait_ge(mm.h, b_mm + 4 * a2)
                        if a2 >= 3:
                            v.wait_ge(st[epr].h, b_st[epr] + 16 * ((a2 - 3) // 2 + 1))
                        v.tensor_scalar(vout[epr][:64, :], ps[epr][:64, :],
                                        p.s["cst"][:64, C_ISQH:C_ISQH + 1], None, ALU.mult).then_inc(ev.h, 1)
                v.wait_ge(mm.h, b_mm + 4 * 128)
                v.wait_ge(st[1].h, b_st[1] + 16 * 63)
                v.tensor_scalar(vout[1][:64, :], ps[1][:64, :],
                                p.s["cst"][:64, C_ISQH:C_ISQH + 1], None, ALU.mult).then_inc(ev.h, 1)
            @b.tensor
            def _(tn):
                for a2 in range(128):
                    pr = a2 % 2
                    tn.wait_ge(cv.h, b_cv + a2 + 1)
                    if a2 >= 2:
                        tn.wait_ge(ev.h, b_ev + a2 - 1)
                    for i in range(4):
                        mi = tn.matmul(ps[pr][:64, i * 512:(i + 1) * 512], h64s[:],
                                       wbbf[pr][:, i * 512:(i + 1) * 512], start=True, stop=True)
                    mi.then_inc(mm.h, 4)
            @b.scalar
            def _(s):
                for a2 in range(128):
                    pr = a2 % 2
                    s.wait_ge(ev.h, b_ev + a2 + 1)
                    s.dma_start(out=p.d["utmp"][a2 * 64:(a2 + 1) * 64, :], in_=vout[pr][:64, :]).then_inc(st[pr].h, 16)
                s.wait_ge(st[0].h, b_st[0] + 16 * 64)
                s.wait_ge(st[1].h, b_st[1] + 16 * 64)
        ldh.v += 32
        ld[0].v += 16 * 64; ld[1].v += 16 * 64
        cv.v += 128
        mm.v += 4 * 128
        ev.v += 128
        st[0].v += 16 * 64; st[1].v += 16 * 64
        b_ld = [ld[0].v, ld[1].v]; b_st = [st[0].v, st[1].v]
        b_cv, b_mm, b_ev = cv.v, mm.v, ev.v
        with nc.Block() as b:
            @b.sync
            def _(sync):
                for bb in range(64):
                    pr = bb % 2
                    if bb >= 2:
                        sync.wait_ge(mm.h, b_mm + 4 * (bb - 1))
                    src = bass.AP(p.d["utmp"].tensor, bb * D, [[64 * D, 128], [1, D]])
                    sync.dma_start(out=ublk[pr][:], in_=src).then_inc(ld[pr].h, 16)
            @b.vector
            def _(v):
                for bb in range(64):
                    pr = bb % 2
                    v.wait_ge(mm.h, b_mm + 4 * (bb + 1))
                    if bb >= 2:
                        v.wait_ge(st[pr].h, b_st[pr] + 16 * ((bb - 2) // 2 + 1))
                    v.tensor_copy(vout[pr][:], ps[pr][:]).then_inc(ev.h, 1)
            @b.tensor
            def _(tn):
                for bb in range(64):
                    pr = bb % 2
                    tn.wait_ge(ld[pr].h, b_ld[pr] + 16 * (bb // 2 + 1))
                    if bb >= 2:
                        tn.wait_ge(ev.h, b_ev + bb - 1)
                    for i in range(4):
                        mi = tn.matmul(ps[pr][:, i * 512:(i + 1) * 512], h128s[:],
                                       ublk[pr][:, i * 512:(i + 1) * 512], start=True, stop=True)
                    mi.then_inc(mm.h, 4)
            @b.scalar
            def _(s):
                for bb in range(64):
                    pr = bb % 2
                    s.wait_ge(ev.h, b_ev + bb + 1)
                    dst = bass.AP(p.d["w2p"].tensor, bb * D, [[64 * D, 128], [1, D]])
                    s.dma_start(out=dst, in_=vout[pr][:]).then_inc(st[pr].h, 16)
                s.wait_ge(st[0].h, b_st[0] + 16 * 32)
                s.wait_ge(st[1].h, b_st[1] + 16 * 32)
        ld[0].v += 16 * 32; ld[1].v += 16 * 32
        mm.v += 4 * 64
        ev.v += 64
        st[0].v += 16 * 32; st[1].v += 16 * 32


def emit_ln_block(p, es, xin_ap, r0, xnT, psT, bufs):
    nc = p.nc
    xsb, stt8, sqd, xnb = bufs["xsb"], bufs["stt8"], bufs["sqd"], bufs["xnb"]
    gapt = bufs["gapt"]
    lx = [p.sem("ln_lx0"), p.sem("ln_lx1")]
    sqs = p.sem("ln_sq"); vard = p.sem("ln_var")
    rsd = p.sem("ln_rstd"); xnd = p.sem("ln_xn"); peT = p.sem("ln_peT"); ev = p.sem("ln_ev")
    b_lx = [lx[0].v, lx[1].v]
    b_sq, b_va, b_rs, b_xn, b_pe, b_ev = sqs.v, vard.v, rsd.v, xnd.v, peT.v, ev.v
    with nc.Block() as b:
        @b.sync
        def _(sync):
            for tt in range(4):
                pr = tt % 2
                if tt >= 2:
                    sync.wait_ge(xnd.h, b_xn + tt - 1)
                sync.dma_start(out=xsb[pr][:], in_=xin_ap[r0 + tt * 128:r0 + (tt + 1) * 128, :]).then_inc(lx[pr].h, 16)
        @b.scalar
        def _(s):
            for tt in range(4):
                pr = tt % 2
                s.wait_ge(lx[pr].h, b_lx[pr] + 16 * (tt // 2 + 1))
                if tt >= 1:
                    s.wait_ge(vard.h, b_va + tt)
                s.activation(out=sqd[:], in_=xsb[pr][:], func=AF.Square).then_inc(sqs.h, 1)
                s.wait_ge(vard.h, b_va + tt + 1)
                s.activation(out=stt8[:, pr * 16 + 12:pr * 16 + 13], in_=stt8[:, pr * 16 + 4:pr * 16 + 5],
                             func=AF.Sqrt, bias=cc(p, C_EPSLN)).then_inc(rsd.h, 1)
        @b.vector
        def _(v):
            v.memset(gapt[0][:], 0.0)
            v.memset(gapt[1][:], 0.0)
            for tt in range(4):
                pr = tt % 2
                c_ = lambda k: stt8[:, pr * 16 + k:pr * 16 + k + 1]
                v.wait_ge(lx[pr].h, b_lx[pr] + 16 * (tt // 2 + 1))
                v.reduce_sum(c_(0), xsb[pr][:], axis=AX)
                v.wait_ge(sqs.h, b_sq + tt + 1)
                v.reduce_sum(c_(1), sqd[:], axis=AX)
                v.tensor_scalar(c_(2), c_(0), cc(p, C_INV2048), None, ALU.mult)
                v.tensor_scalar(c_(5), c_(1), cc(p, C_INV2048), None, ALU.mult)
                v.tensor_mul(c_(3), c_(2), c_(2))
                v.tensor_copy(gapt[0][:], gapt[1][:])
                v.tensor_sub(c_(4), c_(5), c_(3)).then_inc(vard.h, 1)
                v.wait_ge(rsd.h, b_rs + tt + 1)
                v.reciprocal(c_(6), c_(12))
                if tt >= 2:
                    v.wait_ge(peT.h, b_pe + 16 * (tt - 1))
                v.tensor_copy(gapt[0][:], gapt[1][:])
                v.tensor_scalar(xnb[pr][:], xsb[pr][:], c_(2), c_(6),
                                ALU.subtract, ALU.mult).then_inc(xnd.h, 1)
                for c in range(16):
                    idx = tt * 16 + c
                    v.wait_ge(peT.h, b_pe + idx + 1)
                    v.tensor_copy(xnT[:, c, tt * 128:(tt + 1) * 128], psT[idx % 2][:]).then_inc(ev.h, 1)
        @b.tensor
        def _(tn):
            for tt in range(4):
                pr = tt % 2
                tn.wait_ge(xnd.h, b_xn + tt + 1)
                for c in range(16):
                    idx = tt * 16 + c
                    if idx >= 2:
                        tn.wait_ge(ev.h, b_ev + idx - 1)
                    tn.transpose(psT[idx % 2][:], xnb[pr][:, c * 128:(c + 1) * 128],
                                 p.s["ident"][:]).then_inc(peT.h, 1)
    lx[0].v += 16 * 2; lx[1].v += 16 * 2
    sqs.v += 4
    vard.v += 4
    rsd.v += 4
    xnd.v += 4
    peT.v += 64
    ev.v += 64


def _load_w_slab(sync, p, dram_ap, h0, hw, dst, semh):
    ncols = dram_ap.tensor.shape[1]
    src = bass.AP(dram_ap.tensor, h0, [[ncols, 128], [128 * ncols, 16], [1, hw]])
    sync.dma_start(out=dst[:], in_=src).then_inc(semh, 16)


def emit_ffn_h_l0(p, es, xnT, h_all, psG, psU, bufs):
    nc = p.nc
    ws1, ws3, gsc = bufs["ws1"], bufs["ws3"], bufs["gsc"]
    lw = [p.sem("h0_lw0"), p.sem("h0_lw1")]
    mmh = p.sem("h0_mm"); sact = p.sem("h0_act"); hdone = p.sem("h0_hd")
    b_lw = [lw[0].v, lw[1].v]
    b_mm, b_ac, b_hd = mmh.v, sact.v, hdone.v
    with nc.Block() as b:
        @b.sync
        def _(sync):
            for s in range(32):
                pr = s % 2
                if s >= 2:
                    sync.wait_ge(mmh.h, b_mm + 2 * s - 2)
                _load_w_slab(sync, p, p.d["w1b"], s * 256, 256, ws1[pr], lw[pr].h)
                _load_w_slab(sync, p, p.d["w3b"], s * 256, 256, ws3[pr], lw[pr].h)
        @b.tensor
        def _(tn):
            for hb in range(64):
                s, j = hb // 2, hb % 2
                spr, ph = s % 2, hb % 2
                tn.wait_ge(lw[spr].h, b_lw[spr] + 32 * (s // 2 + 1))
                if hb >= 2:
                    tn.wait_ge(hdone.h, b_hd + hb - 1)
                for c in range(16):
                    tn.matmul(psG[ph][:], ws1[spr][:, c, j * 128:(j + 1) * 128],
                              xnT[:, c, :], start=(c == 0), stop=(c == 15))
                for c in range(16):
                    mi = tn.matmul(psU[ph][:], ws3[spr][:, c, j * 128:(j + 1) * 128],
                                   xnT[:, c, :], start=(c == 0), stop=(c == 15))
                mi.then_inc(mmh.h, 1)
        @b.scalar
        def _(s):
            for hb in range(64):
                ph = hb % 2
                s.wait_ge(mmh.h, b_mm + hb + 1)
                if hb >= 2:
                    s.wait_ge(hdone.h, b_hd + hb - 1)
                s.activation(out=gsc[ph][:], in_=psG[ph][:], func=AF.Silu).then_inc(sact.h, 1)
        @b.vector
        def _(v):
            for hb in range(64):
                ph = hb % 2
                v.wait_ge(sact.h, b_ac + hb + 1)
                v.tensor_mul(h_all[:, hb, :], gsc[ph][:], psU[ph][:]).then_inc(hdone.h, 1)
    lw[0].v += 32 * 16; lw[1].v += 32 * 16
    mmh.v += 64
    sact.v += 64
    hdone.v += 64


def emit_ffn_h_l1(p, es, xnT, h_all, psG, psU, psT, bufs):
    nc = p.nc
    wsg, wsu, gsc = bufs["wsg"], bufs["wsu"], bufs["gsc"]
    hmb, sqb, htm = bufs["hmb"], bufs["sqb"], bufs["htm"]
    ta, tb, tcs, td, tge = bufs["ta"], bufs["tb"], bufs["tc"], bufs["td"], bufs["tge"]
    gapq = bufs["gapq"]
    ta_, tb_, tcs_, td_ = ta, tb, tcs, td
    lw = [p.sem("h1_lw0"), p.sem("h1_lw1")]
    mmh = p.sem("h1_mm"); sact = p.sem("h1_act"); sqd = p.sem("h1_sq")
    hmf = p.sem("h1_hmf"); mkd = p.sem("h1_mkd"); peT = p.sem("h1_peT"); evd = p.sem("h1_evd")
    b_lw = [lw[0].v, lw[1].v]
    b_mm, b_ac, b_sq = mmh.v, sact.v, sqd.v
    b_hm, b_mk, b_pe, b_ev = hmf.v, mkd.v, peT.v, evd.v
    with nc.Block() as b:
        @b.sync
        def _(sync):
            for hs in range(16):
                pr = hs % 2
                if hs >= 2:
                    sync.wait_ge(mmh.h, b_mm + 4 * (hs - 1))
                _load_w_slab(sync, p, p.d["wg1b"], hs * 512, 512, wsg[pr], lw[pr].h)
                _load_w_slab(sync, p, p.d["wu1b"], hs * 512, 512, wsu[pr], lw[pr].h)
        @b.tensor
        def _(tn):
            for hs in range(16):
                spr = hs % 2
                tn.wait_ge(lw[spr].h, b_lw[spr] + 32 * (hs // 2 + 1))
                for tc in range(4):
                    idx = hs * 4 + tc
                    ph = idx % 2
                    if idx >= 2:
                        tn.wait_ge(hmf.h, b_hm + idx - 1)
                    for c in range(16):
                        tn.matmul(psG[ph][:], xnT[:, c, tc * 128:(tc + 1) * 128],
                                  wsg[spr][:, c, :], start=(c == 0), stop=(c == 15))
                    for c in range(16):
                        mi = tn.matmul(psU[ph][:], xnT[:, c, tc * 128:(tc + 1) * 128],
                                       wsu[spr][:, c, :], start=(c == 0), stop=(c == 15))
                    mi.then_inc(mmh.h, 1)
                if hs >= 1:
                    tn.wait_ge(mkd.h, b_mk + hs)
                    for tc in range(4):
                        for j in range(4):
                            k = (hs - 1) * 16 + tc * 4 + j
                            if k >= 2:
                                tn.wait_ge(evd.h, b_ev + k - 1)
                            tn.transpose(psT[k % 2][:], htm[(hs - 1) % 2][:, tc, j * 128:(j + 1) * 128],
                                         p.s["ident"][:]).then_inc(peT.h, 1)
            tn.wait_ge(mkd.h, b_mk + 16)
            for tc in range(4):
                for j in range(4):
                    k = 15 * 16 + tc * 4 + j
                    tn.wait_ge(evd.h, b_ev + k - 1)
                    tn.transpose(psT[k % 2][:], htm[1][:, tc, j * 128:(j + 1) * 128],
                                 p.s["ident"][:]).then_inc(peT.h, 1)
        @b.scalar
        def _(s):
            for hs in range(16):
                for tc in range(4):
                    idx = hs * 4 + tc
                    ph = idx % 2
                    s.wait_ge(mmh.h, b_mm + idx + 1)
                    if idx >= 2:
                        s.wait_ge(hmf.h, b_hm + idx - 1)
                    s.activation(out=gsc[ph][:], in_=psG[ph][:], func=AF.Silu).then_inc(sact.h, 1)
                s.wait_ge(hmf.h, b_hm + 4 * (hs + 1))
                if hs >= 1:
                    s.wait_ge(mkd.h, b_mk + hs)
                s.activation(out=sqb[:], in_=hmb[:], func=AF.Square).then_inc(sqd.h, 1)
        @b.vector
        def _(v):
            v.memset(gapq[0], 0.0)
            v.memset(gapq[1], 0.0)
            for hs in range(16):
                pr = hs % 2
                for tc in range(4):
                    idx = hs * 4 + tc
                    ph = idx % 2
                    v.wait_ge(sact.h, b_ac + idx + 1)
                    if tc == 0:
                        if hs >= 2:
                            v.wait_ge(peT.h, b_pe + 16 * (hs - 1))
                        if hs >= 1:
                            v.wait_ge(sqd.h, b_sq + hs)
                    v.tensor_mul(hmb[:, tc, :], gsc[ph][:], psU[ph][:]).then_inc(hmf.h, 1)
                v.wait_ge(sqd.h, b_sq + hs + 1)
                sqq = sqb[:].rearrange("p t (g r) -> p t g r", r=4)
                hmq = hmb[:].rearrange("p t (g r) -> p t g r", r=4)
                htq = htm[pr][:].rearrange("p t (g r) -> p t g r", r=4)
                v.tensor_max(ta, sqq[:, :, :, 0], sqq[:, :, :, 1])
                v.tensor_max(tcs, sqq[:, :, :, 2], sqq[:, :, :, 3])
                v.tensor_tensor(tb, sqq[:, :, :, 0], sqq[:, :, :, 1], ALU.min)
                v.tensor_tensor(td, sqq[:, :, :, 2], sqq[:, :, :, 3], ALU.min)
                v.tensor_tensor(ta, ta, tcs, ALU.min)
                v.tensor_max(tb, tb, td)
                v.tensor_copy(gapq[0], gapq[1])
                v.tensor_max(ta, ta, tb)
                v.tensor_copy(gapq[0], gapq[1])
                tge0, tge1 = tge
                v.tensor_tensor(tge0, sqq[:, :, :, 0], ta, ALU.is_ge)
                v.tensor_tensor(tge1, sqq[:, :, :, 1], ta, ALU.is_ge)
                v.tensor_mul(htq[:, :, :, 0], hmq[:, :, :, 0], tge0)
                v.tensor_tensor(tge0, sqq[:, :, :, 2], ta, ALU.is_ge)
                v.tensor_mul(htq[:, :, :, 1], hmq[:, :, :, 1], tge1)
                v.tensor_tensor(tge1, sqq[:, :, :, 3], ta, ALU.is_ge)
                v.tensor_mul(htq[:, :, :, 2], hmq[:, :, :, 2], tge0)
                v.tensor_copy(gapq[0], gapq[1])
                mi = v.tensor_mul(htq[:, :, :, 3], hmq[:, :, :, 3], tge1)
                mi.then_inc(mkd.h, 1)
                if hs >= 1:
                    for tc in range(4):
                        for j in range(4):
                            k = (hs - 1) * 16 + tc * 4 + j
                            v.wait_ge(peT.h, b_pe + k + 1)
                            v.tensor_copy(h_all[:, (hs - 1) * 4 + j, tc * 128:(tc + 1) * 128],
                                          psT[k % 2][:]).then_inc(evd.h, 1)
            for tc in range(4):
                for j in range(4):
                    k = 15 * 16 + tc * 4 + j
                    v.wait_ge(peT.h, b_pe + k + 1)
                    v.tensor_copy(h_all[:, 15 * 4 + j, tc * 128:(tc + 1) * 128],
                                  psT[k % 2][:]).then_inc(evd.h, 1)
    lw[0].v += 32 * 8; lw[1].v += 32 * 8
    mmh.v += 64
    sact.v += 64
    sqd.v += 16
    hmf.v += 64
    mkd.v += 16
    peT.v += 256
    evd.v += 256


def emit_ffn_down(p, es, wdram, xin_ap, xout_ap, r0, psY, h_all, bufs):
    nc = p.nc
    w2sb, xr, res = bufs["w2sb"], bufs["xr"], bufs["res"]
    lw = [p.sem("dn_lw0"), p.sem("dn_lw1")]
    lx = [p.sem("dn_lx0"), p.sem("dn_lx1")]
    sxo = [p.sem("dn_sxo0"), p.sem("dn_sxo1")]
    mmd = p.sem("dn_mmd"); resd = p.sem("dn_res")
    b_lw = [lw[0].v, lw[1].v]; b_lx = [lx[0].v, lx[1].v]; b_sx = [sxo[0].v, sxo[1].v]
    b_md, b_re = mmd.v, resd.v
    with nc.Block() as b:
        @b.sync
        def _(sync):
            for fc in range(4):
                for s in range(8):
                    g = fc * 8 + s
                    pr = g % 2
                    if g >= 2:
                        sync.wait_ge(mmd.h, b_md + g - 1)
                    src = bass.AP(wdram.tensor, (s * 1024) * D + fc * 512,
                                  [[D, 128], [128 * D, 8], [1, 512]])
                    sync.dma_start(out=w2sb[pr][:], in_=src).then_inc(lw[pr].h, 16)
        @b.scalar
        def _(s):
            for idx in range(18):
                if idx < 16:
                    fc, tc = idx // 4, idx % 4
                    pr = idx % 2
                    if idx >= 2:
                        s.wait_ge(resd.h, b_re + idx - 1)
                    s.dma_start(out=xr[pr][:],
                                in_=xin_ap[r0 + tc * 128:r0 + (tc + 1) * 128,
                                           fc * 512:(fc + 1) * 512]).then_inc(lx[pr].h, 16)
                if idx >= 2:
                    j = idx - 2
                    fcj, tcj = j // 4, j % 4
                    prj = j % 2
                    s.wait_ge(resd.h, b_re + j + 1)
                    s.dma_start(out=xout_ap[r0 + tcj * 128:r0 + (tcj + 1) * 128,
                                            fcj * 512:(fcj + 1) * 512], in_=res[prj][:]).then_inc(sxo[prj].h, 16)
            s.wait_ge(sxo[0].h, b_sx[0] + 16 * 8)
            s.wait_ge(sxo[1].h, b_sx[1] + 16 * 8)
        @b.tensor
        def _(tn):
            for fc in range(4):
                if fc >= 1:
                    tn.wait_ge(resd.h, b_re + 4 * fc)
                for s in range(8):
                    g = fc * 8 + s
                    pr = g % 2
                    tn.wait_ge(lw[pr].h, b_lw[pr] + 16 * (g // 2 + 1))
                    for hbs in range(8):
                        hb = s * 8 + hbs
                        for tc in range(4):
                            mi = tn.matmul(psY[tc][:], h_all[:, hb, tc * 128:(tc + 1) * 128],
                                           w2sb[pr][:, hbs, :], start=(hb == 0), stop=(hb == 63))
                    mi.then_inc(mmd.h, 1)
        @b.vector
        def _(v):
            for fc in range(4):
                v.wait_ge(mmd.h, b_md + 8 * (fc + 1))
                for tc in range(4):
                    idx = fc * 4 + tc
                    pr = idx % 2
                    v.wait_ge(lx[pr].h, b_lx[pr] + 16 * (idx // 2 + 1))
                    if idx >= 2:
                        v.wait_ge(sxo[pr].h, b_sx[pr] + 16 * (idx // 2))
                    v.scalar_tensor_tensor(res[pr][:], psY[tc][:], cc(p, C_HALF), xr[pr][:],
                                           ALU.mult, ALU.add).then_inc(resd.h, 1)
    lw[0].v += 16 * 16; lw[1].v += 16 * 16
    lx[0].v += 16 * 8; lx[1].v += 16 * 8
    mmd.v += 32
    resd.v += 16
    sxo[0].v += 16 * 8; sxo[1].v += 16 * 8


def emit_ffn(p, layer, xin_ap, xout_ap, blocks=range(4)):
    with contextlib.ExitStack() as es:
        xnT = p.sbuf(es, "xnT", [128, 16, 512], BF16)
        h_all = p.sbuf(es, "h_all", [128, 64, 512], BF16)
        psT = [p.psum(es, "psT", [128, 128], BF16) for _ in range(2)]
        psG = [p.psum(es, "psG", [128, 512], F32) for _ in range(2)]
        psU = [p.psum(es, "psU", [128, 512], F32) for _ in range(2)]
        lnb = {
            "xsb": [p.sbuf(es, "xsb", [128, D], F32) for _ in range(2)],
            "stt8": p.sbuf(es, "stt8", [128, 32], F32),
            "sqd": p.sbuf(es, "sqd", [128, D], F32),
            "xnb": [p.sbuf(es, "xnb", [128, D], BF16) for _ in range(2)],
            "gapt": [p.sbuf(es, "gapt", [128, 4], F32) for _ in range(2)],
        }
        if layer == "l0":
            hb_bufs = {
                "ws1": [p.sbuf(es, "ws1", [128, 16, 256], BF16) for _ in range(2)],
                "ws3": [p.sbuf(es, "ws3", [128, 16, 256], BF16) for _ in range(2)],
                "gsc": [p.sbuf(es, "gsc", [128, 512], F32) for _ in range(2)],
            }
            wdram = p.d["w2p"]
        else:
            hb_bufs = {
                "wsg": [p.sbuf(es, "wsg", [128, 16, 512], BF16) for _ in range(2)],
                "wsu": [p.sbuf(es, "wsu", [128, 16, 512], BF16) for _ in range(2)],
                "gsc": [p.sbuf(es, "gsc", [128, 512], F32) for _ in range(2)],
                "hmb": p.sbuf(es, "hmb", [128, 4, 512], F32),
                "sqb": p.sbuf(es, "sqb", [128, 4, 512], F32),
                "htm": [p.sbuf(es, "htm", [128, 4, 512], BF16) for _ in range(2)],
            }
            _sqv = lnb["sqd"][:].rearrange("p (a b) -> p a b", a=4)
            _xv = lnb["xsb"][0][:].rearrange("p (a b) -> p a b", a=4)
            hb_bufs["ta"] = _sqv[:, :, 0:128]
            hb_bufs["tb"] = _sqv[:, :, 128:256]
            hb_bufs["tc"] = _sqv[:, :, 256:384]
            hb_bufs["td"] = _sqv[:, :, 384:512]
            hb_bufs["tge"] = [_xv[:, :, 0:128], _xv[:, :, 128:256]]
            hb_bufs["gapq"] = [_xv[:, 0, 256:260], _xv[:, 0, 260:264]]
            wdram = p.d["wd1b"]
        if layer == "l0":
            dn_bufs = {
                "w2sb": [p.sbuf(es, "w2sb", [128, 8, 512], BF16) for _ in range(2)],
                "xr": [p.sbuf(es, "xr", [128, 512], F32) for _ in range(2)],
                "res": [p.sbuf(es, "res", [128, 512], F32) for _ in range(2)],
            }
        else:
            hmb = hb_bufs["hmb"]
            dn_bufs = {
                "w2sb": [hb_bufs["wsg"][0][:, 0:8, :], hb_bufs["wsg"][1][:, 0:8, :]],
                "xr": [hmb[:, 0, :], hmb[:, 1, :]],
                "res": [hmb[:, 2, :], hmb[:, 3, :]],
            }
        for blk in blocks:
            r0 = blk * 512
            emit_ln_block(p, es, xin_ap, r0, xnT, psT, lnb)
            if layer == "l0":
                emit_ffn_h_l0(p, es, xnT, h_all, psG, psU, hb_bufs)
            else:
                emit_ffn_h_l1(p, es, xnT, h_all, psG, psU, psT, hb_bufs)
            emit_ffn_down(p, es, wdram, xin_ap, xout_ap, r0, [psG[0], psG[1], psU[0], psU[1]],
                          h_all, dn_bufs)


def emit_down_generic(p, es, wdram, xin_ap, xout_ap, r0, psY, lhsT_list, scale_idx, bufs):
    nc = p.nc
    n_k = len(lhsT_list)
    n_s = n_k // 8
    w2sb, xr, res = bufs["w2sb"], bufs["xr"], bufs["res"]
    lw = [p.sem("dn_lw0"), p.sem("dn_lw1")]
    lx = [p.sem("dn_lx0"), p.sem("dn_lx1")]
    sxo = [p.sem("dn_sxo0"), p.sem("dn_sxo1")]
    mmd = p.sem("dn_mmd"); resd = p.sem("dn_res")
    b_lw = [lw[0].v, lw[1].v]; b_lx = [lx[0].v, lx[1].v]; b_sx = [sxo[0].v, sxo[1].v]
    b_md, b_re = mmd.v, resd.v
    with nc.Block() as b:
        @b.sync
        def _(sync):
            for fc in range(4):
                for s in range(n_s):
                    g = fc * n_s + s
                    pr = g % 2
                    if g >= 2:
                        sync.wait_ge(mmd.h, b_md + g - 1)
                    src = bass.AP(wdram.tensor, (s * 1024) * D + fc * 512,
                                  [[D, 128], [128 * D, 8], [1, 512]])
                    sync.dma_start(out=w2sb[pr], in_=src).then_inc(lw[pr].h, 16)
        @b.scalar
        def _(s):
            for idx in range(18):
                if idx < 16:
                    fc, tc = idx // 4, idx % 4
                    pr = idx % 2
                    if idx >= 2:
                        s.wait_ge(resd.h, b_re + idx - 1)
                    s.dma_start(out=xr[pr], in_=xin_ap[r0 + tc * 128:r0 + (tc + 1) * 128,
                                                       fc * 512:(fc + 1) * 512]).then_inc(lx[pr].h, 16)
                if idx >= 2:
                    j = idx - 2
                    fcj, tcj = j // 4, j % 4
                    prj = j % 2
                    s.wait_ge(resd.h, b_re + j + 1)
                    s.dma_start(out=xout_ap[r0 + tcj * 128:r0 + (tcj + 1) * 128,
                                            fcj * 512:(fcj + 1) * 512], in_=res[prj]).then_inc(sxo[prj].h, 16)
            s.wait_ge(sxo[0].h, b_sx[0] + 16 * 8)
            s.wait_ge(sxo[1].h, b_sx[1] + 16 * 8)
        @b.tensor
        def _(tn):
            for fc in range(4):
                if fc >= 1:
                    tn.wait_ge(resd.h, b_re + 4 * fc)
                for s in range(n_s):
                    g = fc * n_s + s
                    pr = g % 2
                    tn.wait_ge(lw[pr].h, b_lw[pr] + 16 * (g // 2 + 1))
                    for ks in range(8):
                        k = s * 8 + ks
                        for tc in range(4):
                            mi = tn.matmul(psY[tc][:], lhsT_list[k][tc],
                                           w2sb[pr][:, ks, :], start=(k == 0), stop=(k == n_k - 1))
                    mi.then_inc(mmd.h, 1)
        @b.vector
        def _(v):
            for fc in range(4):
                v.wait_ge(mmd.h, b_md + n_s * (fc + 1))
                for tc in range(4):
                    idx = fc * 4 + tc
                    pr = idx % 2
                    v.wait_ge(lx[pr].h, b_lx[pr] + 16 * (idx // 2 + 1))
                    if idx >= 2:
                        v.wait_ge(sxo[pr].h, b_sx[pr] + 16 * (idx // 2))
                    v.scalar_tensor_tensor(res[pr], psY[tc][:], cc(p, scale_idx), xr[pr],
                                           ALU.mult, ALU.add).then_inc(resd.h, 1)
    lw[0].v += 16 * (2 * n_s); lw[1].v += 16 * (2 * n_s)
    lx[0].v += 16 * 8; lx[1].v += 16 * 8
    mmd.v += 4 * n_s
    resd.v += 16
    sxo[0].v += 16 * 8; sxo[1].v += 16 * 8


def emit_attn_l1(p, xin_ap, xout_ap):
    nc = p.nc
    with contextlib.ExitStack() as es:
        xnT = p.sbuf(es, "xnT", [128, 16, 512], BF16)
        kT_buf = p.sbuf(es, "kT", [128, 8, T], BF16)
        v_buf = p.sbuf(es, "vb", [128, 16, 8, 128], BF16)
        outT = p.sbuf(es, "outT", [128, 16, 16, 128], BF16)
        tril_sb = p.sbuf(es, "tril_sb", [128, 128], F32)
        psT = [p.psum(es, "psT", [128, 128], BF16) for _ in range(2)]
        psA = [p.psum(es, "psA", [128, 512], F32) for _ in range(2)]
        psB = [p.psum(es, "psB", [128, 512], F32) for _ in range(2)]
        psO = p.psum(es, "psO", [128, 128], F32)
        lnb = {
            "xsb": [p.sbuf(es, "xsb", [128, D], F32) for _ in range(2)],
            "stt8": p.sbuf(es, "stt8", [128, 32], F32),
            "sqd": p.sbuf(es, "sqd", [128, D], F32),
            "xnb": [p.sbuf(es, "xnb", [128, D], BF16) for _ in range(2)],
            "gapt": [p.sbuf(es, "gapt", [128, 4], F32) for _ in range(2)],
        }
        wsl = [p.sbuf(es, "wsl", [128, 16, 128], BF16) for _ in range(2)]
        qsc = p.sbuf(es, "qsc", [128, 512], BF16)
        qt_sb = p.sbuf(es, "qt_sb", [128, 8, 128], BF16)
        pT_buf = p.sbuf(es, "pT_buf", [128, 17, 128], BF16)
        sc_sb = lnb["xsb"][0][:]
        p_bf = lnb["xsb"][1][:].bitcast(BF16)[:, 0:2048]
        scol = p.sbuf(es, "scol", [128, 16], F32)
        E_bf = p.sbuf(es, "E_bf", [128, 128], BF16)
        o_sb = p.sbuf(es, "o_sb", [128, 128], BF16)
        dnb = {
            "w2sb": [kT_buf[:, :, 0:512], kT_buf[:, :, 512:1024]],
            "xr": [lnb["sqd"][:, 0:512], lnb["sqd"][:, 512:1024]],
            "res": [lnb["sqd"][:, 1024:1536], lnb["sqd"][:, 1536:2048]],
        }
        lw = [p.sem("a1_lw0"), p.sem("a1_lw1")]
        ltr = p.sem("a1_ltr")
        mmq = p.sem("a1_mmq"); evq = p.sem("a1_evq")
        stq = [p.sem("a1_stq0"), p.sem("a1_stq1")]
        lqt = p.sem("a1_lqt"); mms = p.sem("a1_mms"); sft = p.sem("a1_sft")
        act = p.sem("a1_act"); peT = p.sem("a1_peT"); evT = p.sem("a1_evT")
        with nc.Block() as b:
            @b.sync
            def _(sync):
                sync.dma_start(out=tril_sb[:], in_=p.d["tril"][:, :]).then_inc(ltr.h, 16)
                sync.wait_ge(ltr.h, ltr.v + 16)
        ltr.v += 16
        for hg in range(2):
            heads = list(range(hg * 8, hg * 8 + 8))
            for blk in range(4):
                emit_ln_block(p, es, xin_ap, blk * 512, xnT, psT, lnb)
                b_lw = [lw[0].v, lw[1].v]
                b_mmq, b_evq, b_stq = mmq.v, evq.v, [stq[0].v, stq[1].v]
                b_peT, b_evT = peT.v, evT.v
                with nc.Block() as b:
                    @b.sync
                    def _(sync):
                        for i in range(8):
                            for w_i, wname in enumerate(("wq1b", "wk1b", "wv1b")):
                                j = i * 3 + w_i
                                pr = j % 2
                                if j >= 2:
                                    sync.wait_ge(mmq.h, b_mmq + j - 1)
                                _load_w_slab(sync, p, p.d[wname], heads[i] * 128, 128, wsl[pr], lw[pr].h)
                    @b.tensor
                    def _(tn):
                        for i in range(8):
                            for w_i in range(3):
                                j = i * 3 + w_i
                                pr = j % 2
                                ps = [psA[0], psB[0], psA[1]][w_i] if i % 2 == 0 else [psB[1], psA[0] if False else psB[0], psA[1]][0:3][w_i]
                                mi = None
                            for w_i in range(3):
                                j = i * 3 + w_i
                                pr = j % 2
                                ps = [psA[0], psB[0], psA[1]][w_i]
                                tn.wait_ge(lw[pr].h, b_lw[pr] + 16 * (j // 2 + 1))
                                if i >= 1:
                                    tn.wait_ge(evq.h, b_evq + (i - 1) * 3 + w_i + 1)
                                for c in range(16):
                                    mi = tn.matmul(ps[:], wsl[pr][:, c, :], xnT[:, c, :],
                                                   start=(c == 0), stop=(c == 15))
                                mi.then_inc(mmq.h, 1)
                            tn.wait_ge(evq.h, b_evq + i * 3 + 3)
                            for tpc in range(4):
                                k = i * 4 + tpc
                                if k >= 2:
                                    tn.wait_ge(evT.h, b_evT + k - 1)
                                tn.transpose(psT[k % 2][:], qsc[:, tpc * 128:(tpc + 1) * 128],
                                             p.s["ident"][:]).then_inc(peT.h, 1)
                    @b.vector
                    def _(v):
                        for i in range(8):
                            v.wait_ge(mmq.h, b_mmq + i * 3 + 1)
                            if i >= 1:
                                v.wait_ge(peT.h, b_peT + i * 4)
                                v.wait_ge(stq[(i - 1) % 2].h, b_stq[(i - 1) % 2] + 16 * ((i - 1) // 2 + 1))
                            v.tensor_copy(qsc[:], psA[0][:]).then_inc(evq.h, 1)
                            v.wait_ge(mmq.h, b_mmq + i * 3 + 2)
                            v.tensor_copy(kT_buf[:, i, blk * 512:(blk + 1) * 512], psB[0][:]).then_inc(evq.h, 1)
                            v.wait_ge(mmq.h, b_mmq + i * 3 + 3)
                            v.wait_ge(stq[i % 2].h, b_stq[i % 2] + 16 * (i // 2))
                            v.tensor_copy(qsc[:], psA[1][:]).then_inc(evq.h, 1)
                            for tpc in range(4):
                                k = i * 4 + tpc
                                v.wait_ge(peT.h, b_peT + k + 1)
                                v.tensor_copy(v_buf[:, blk * 4 + tpc, i, :], psT[k % 2][:]).then_inc(evT.h, 1)
                    @b.scalar
                    def _(s):
                        for i in range(8):
                            s.wait_ge(evq.h, b_evq + i * 3 + 1)
                            dst = bass.AP(p.d["qtsp"].tensor, heads[i] * 128 * T + blk * 512,
                                          [[T, 128], [1, 512]])
                            s.dma_start(out=dst, in_=qsc[:]).then_inc(stq[i % 2].h, 16)
                        s.wait_ge(stq[0].h, b_stq[0] + 16 * 4)
                        s.wait_ge(stq[1].h, b_stq[1] + 16 * 4)
                lw[0].v += 16 * 12; lw[1].v += 16 * 12
                mmq.v += 24
                evq.v += 24
                stq[0].v += 16 * 4; stq[1].v += 16 * 4
                peT.v += 32
                evT.v += 32
            b_lqt, b_mms, b_sft, b_act = lqt.v, mms.v, sft.v, act.v
            b_peT, b_evT = peT.v, evT.v
            cumT = [0]
            for idx in range(128):
                cumT.append(cumT[-1] + idx // 8 + 3)
            with nc.Block() as b:
                @b.sync
                def _(sync):
                    for qi in range(16):
                        if qi >= 1:
                            sync.wait_ge(mms.h, b_mms + (qi * 8 - 1) * 2 + 1)
                        for i in range(8):
                            src = bass.AP(p.d["qtsp"].tensor, (hg * 8 + i) * 128 * T + qi * 128,
                                          [[T, 128], [1, 128]])
                            sync.dma_start(out=qt_sb[:, i, :], in_=src).then_inc(lqt.h, 16)
                @b.tensor
                def _(tn):
                    for qi in range(16):
                        C = (qi + 1) * 128
                        tn.wait_ge(lqt.h, b_lqt + 16 * 8 * (qi + 1))
                        for i in range(8):
                            idx = qi * 8 + i
                            if idx >= 1:
                                tn.wait_ge(sft.h, b_sft + (idx - 1) * 3 + 1)
                            nch = (C + 511) // 512
                            for sc in range(nch):
                                n0 = sc * 512
                                n1 = min(C, n0 + 512)
                                ps = [psA[0], psA[1], psB[0], psB[1]][sc]
                                mi = tn.matmul(ps[:, 0:n1 - n0], qt_sb[:, i, :],
                                               kT_buf[:, i, n0:n1], start=True, stop=True)
                            mi.then_inc(mms.h, 1)
                            tn.wait_ge(act.h, b_act + idx + 1)
                            kbase = cumT[idx]
                            for j in range(qi + 1):
                                k = kbase + j
                                if k >= 2:
                                    tn.wait_ge(evT.h, b_evT + k - 1)
                                tn.transpose(psT[k % 2][:], p_bf[:, j * 128:(j + 1) * 128],
                                             p.s["ident"][:]).then_inc(peT.h, 1)
                            tn.wait_ge(sft.h, b_sft + idx * 3 + 2)
                            k = kbase + qi + 1
                            if k >= 2:
                                tn.wait_ge(evT.h, b_evT + k - 1)
                            tn.transpose(psT[k % 2][:], E_bf[:], p.s["ident"][:]).then_inc(peT.h, 1)
                            tn.wait_ge(evT.h, b_evT + kbase + qi + 2)
                            if idx >= 1:
                                tn.wait_ge(sft.h, b_sft + (idx - 1) * 3 + 3)
                            for j in range(qi + 1):
                                tn.matmul(psO[:], pT_buf[:, j, :], v_buf[:, j, i, :],
                                          start=(j == 0), stop=False)
                            if qi < 15:
                                for sc in range(qi + 1, 16):
                                    mi = tn.matmul(psO[:], pT_buf[:, 16, :], v_buf[:, sc, i, :],
                                                   start=False, stop=(sc == 15))
                            else:
                                mi = tn.matmul(psO[:], pT_buf[:, 16, :], v_buf[:, 15, i, :],
                                               start=False, stop=True)
                            mi.then_inc(mms.h, 1)
                            tn.wait_ge(sft.h, b_sft + idx * 3 + 3)
                            k = kbase + qi + 2
                            if k >= 2:
                                tn.wait_ge(evT.h, b_evT + k - 1)
                            tn.transpose(psT[k % 2][:], o_sb[:], p.s["ident"][:]).then_inc(peT.h, 1)
                @b.vector
                def _(v):
                    v.memset(lnb["gapt"][0][:], 0.0)
                    v.memset(lnb["gapt"][1][:], 0.0)
                    gap = lambda: v.tensor_copy(lnb["gapt"][0][:], lnb["gapt"][1][:])
                    for qi in range(16):
                        C = (qi + 1) * 128
                        for i in range(8):
                            idx = qi * 8 + i
                            kbase = cumT[idx]
                            v.wait_ge(mms.h, b_mms + idx * 2 + 1)
                            if idx >= 1:
                                v.wait_ge(act.h, b_act + idx)
                            nch = (C + 511) // 512
                            for sc in range(nch):
                                n0 = sc * 512
                                n1 = min(C, n0 + 512)
                                ps = [psA[0], psA[1], psB[0], psB[1]][sc]
                                if n1 == C:
                                    if n1 - n0 > 128:
                                        v.tensor_scalar(sc_sb[:, n0:n1 - 128], ps[:, 0:n1 - n0 - 128],
                                                        cc(p, C_ISQT), None, ALU.mult)
                                    v.scalar_tensor_tensor(sc_sb[:, C - 128:C], ps[:, n1 - n0 - 128:n1 - n0],
                                                           cc(p, C_ISQT), tril_sb[:], ALU.mult, ALU.mult)
                                else:
                                    v.tensor_scalar(sc_sb[:, n0:n1], ps[:, 0:n1 - n0], cc(p, C_ISQT), None, ALU.mult)
                            gap()
                            v.reduce_max(scol[:, 0:1], sc_sb[:, 0:C], axis=AX, negate=True)
                            gap()
                            if qi < 15:
                                v.tensor_scalar(scol[:, 1:2], scol[:, 0:1], cc(p, C_ZERO), None, ALU.min)
                            else:
                                v.tensor_copy(scol[:, 1:2], scol[:, 0:1])
                            gap()
                            mi = v.tensor_copy(scol[:, 2:3], scol[:, 1:2])
                            mi.then_inc(sft.h, 1)
                            v.wait_ge(act.h, b_act + idx + 1)
                            v.scalar_tensor_tensor(scol[:, 4:5], scol[:, 8:9], cc(p, C_CNT0 + qi),
                                                   scol[:, 12:13], ALU.mult, ALU.add)
                            if idx >= 1:
                                v.wait_ge(peT.h, b_peT + cumT[idx - 1] + (idx - 1) // 8 + 2)
                            v.tensor_scalar(E_bf[:], p.s["ones_bf"][:, 0:128], scol[:, 8:9], None, ALU.mult)
                            mi = v.reciprocal(scol[:, 5:6], scol[:, 4:5])
                            mi.then_inc(sft.h, 1)
                            for j in range(qi + 2):
                                k = kbase + j
                                v.wait_ge(peT.h, b_peT + k + 1)
                                v.tensor_copy(pT_buf[:, j if j <= qi else 16, :], psT[k % 2][:]).then_inc(evT.h, 1)
                            v.wait_ge(mms.h, b_mms + idx * 2 + 2)
                            if idx >= 1:
                                v.wait_ge(peT.h, b_peT + cumT[idx - 1] + (idx - 1) // 8 + 3)
                            mi = v.tensor_scalar(o_sb[:], psO[:], scol[:, 5:6], None, ALU.mult)
                            mi.then_inc(sft.h, 1)
                            k = kbase + qi + 2
                            v.wait_ge(peT.h, b_peT + k + 1)
                            v.tensor_copy(outT[:, hg * 8 + i, qi, :], psT[k % 2][:]).then_inc(evT.h, 1)
                @b.scalar
                def _(s):
                    for qi in range(16):
                        C = (qi + 1) * 128
                        for i in range(8):
                            idx = qi * 8 + i
                            s.wait_ge(sft.h, b_sft + idx * 3 + 1)
                            if idx >= 1:
                                s.wait_ge(peT.h, b_peT + cumT[idx - 1] + (idx - 1) // 8 + 1)
                            s.activation(out=p_bf[:, 0:C], in_=sc_sb[:, 0:C], func=AF.Exp,
                                         bias=scol[:, 2:3], accum_out=scol[:, 12:13])
                            mi = s.activation(out=scol[:, 8:9], in_=scol[:, 2:3], func=AF.Exp)
                            mi.then_inc(act.h, 1)
            lqt.v += 16 * 8 * 16
            mms.v += 2 * 128
            act.v += 128
            sft.v += 3 * 128
            peT.v += cumT[128]
            evT.v += cumT[128]
        psY = [psA[0], psA[1], psB[0], psB[1]]
        for blk in range(4):
            lhsT_list = [[outT[:, h, blk * 4 + tc, :] for tc in range(4)] for h in range(16)]
            emit_down_generic(p, es, p.d["wo1b"], xin_ap, xout_ap, blk * 512, psY,
                              lhsT_list, C_ONE, dnb)


def emit_lightning(p, xin_ap, xout_ap):
    nc = p.nc
    with contextlib.ExitStack() as es:
        xnT = p.sbuf(es, "xnT", [128, 16, T], BF16)
        o_buf = p.sbuf(es, "o_buf", [128, 16, 16, 128], BF16)
        lnb = {
            "xsb": [p.sbuf(es, "xsb", [128, D], F32) for _ in range(2)],
            "stt8": p.sbuf(es, "stt8", [128, 32], F32),
            "sqd": p.sbuf(es, "sqd", [128, D], F32),
            "xnb": [p.sbuf(es, "xnb", [128, D], BF16) for _ in range(2)],
            "gapt": [p.sbuf(es, "gapt", [128, 4], F32) for _ in range(2)],
        }
        wsl = [p.sbuf(es, "wsl", [128, 16, 128], BF16) for _ in range(2)]
        qT_sb = lnb["sqd"][:].bitcast(BF16)[:, 0:2048]
        kT_sb = lnb["xsb"][1][:].bitcast(BF16)[:, 0:2048]
        vT_sb = lnb["xsb"][0][:].bitcast(BF16)[:, 0:2048]
        kd_tm = p.sbuf(es, "kd_tm", [128, 16, 128], BF16)
        v_tm = p.sbuf(es, "v_tm", [128, 16, 128], BF16)
        qd_bf = p.sbuf(es, "qd_bf", [128, 256], BF16)
        qkd_bf = p.sbuf(es, "qkd_bf", [128, 2, 256], BF16)
        qdec_bf = p.sbuf(es, "qdec_bf", [128, 256], BF16)
        ddT = p.sbuf(es, "ddT", [128, 2, 256], F32)
        idxc_sb = p.sbuf(es, "idxc_sb", [128, 2, 256], F32)
        maskd_sb = p.sbuf(es, "maskd_sb", [128, 2, 256], F32)
        arr_sb = p.sbuf(es, "arr_sb", [128, 256], F32)
        slope_sb = p.sbuf(es, "slope_sb", [128, 16], F32)
        dcol = p.sbuf(es, "dcol", [128, 16], F32)
        kv = p.sbuf(es, "kv", [128, 128], F32)
        kv_bf = p.sbuf(es, "kv_bf", [128, 128], BF16)
        ssqcols = p.sbuf(es, "ssqc", [128, 16, 16], F32)
        sqo = p.sbuf(es, "sqo", [128, 2, 128], F32)
        rstd16 = p.sbuf(es, "rstd16", [128, 16], F32)
        sq16 = p.sbuf(es, "sq16", [128, 16], F32)
        gsig = p.sbuf(es, "gsig", [128, 512], F32)
        dtmp = gsig[:].rearrange("p (a b) -> p a b", a=2)
        gt_bf = p.sbuf(es, "gt_bf", [128, 512], BF16)
        wgl = p.sbuf(es, "wgl", [128, 16, 512], BF16)
        gT_sb = qkd_bf[:].rearrange("p a (c d) -> p (a c) d", c=2)
        gT_buf = xnT[:].rearrange("p c (a b) -> p c a b", a=16)[:, :, 0:4, :]
        es2 = contextlib.ExitStack()
        psT = [p.psum(es2, "psT", [128, 128], BF16) for _ in range(2)]
        psQ = [p.psum(es2, "psQ", [128, 512], F32) for _ in range(2)]
        psO2 = [p.psum(es2, "psO2", [128, 128], F32) for _ in range(2)]
        psD = p.psum(es2, "psD", [128, 128], F32)
        dnb = {
            "w2sb": [wgl[:][:, 0:8, :], wgl[:][:, 8:16, :]],
            "xr": [lnb["sqd"][:, 0:512], lnb["sqd"][:, 512:1024]],
            "res": [lnb["sqd"][:, 1024:1536], lnb["sqd"][:, 1536:2048]],
        }
        ld0 = p.sem("la_ld0")
        lw = [p.sem("la_lw0"), p.sem("la_lw1")]
        mmq = p.sem("la_mmq"); evq = p.sem("la_evq"); peT = p.sem("la_peT"); evT = p.sem("la_evT")
        acd = p.sem("la_acd")
        dvd = p.sem("la_dvd")
        mms = p.sem("la_mms")
        stg = [p.sem("la_stg0"), p.sem("la_stg1")]
        lgt = p.sem("la_lgt")
        with nc.Block() as b:
            @b.sync
            def _(sync):
                sync.dma_start(out=idxc_sb[:], in_=p.d["idxc"][:, :, :]).then_inc(ld0.h, 16)
                sync.dma_start(out=maskd_sb[:], in_=p.d["maskd"][:, :, :]).then_inc(ld0.h, 16)
                sync.dma_start(out=arr_sb[:], in_=p.d["arr256"][:, :]).then_inc(ld0.h, 16)
                sync.dma_start(out=slope_sb[:], in_=p.d["slope"][:, :]).then_inc(ld0.h, 16)
                sync.dma_start(out=dcol[:, 12:14], in_=p.d["rev"][:, :]).then_inc(ld0.h, 16)
                sync.wait_ge(ld0.h, ld0.v + 80)
        ld0.v += 80
        for blk in range(4):
            emit_ln_block(p, es, xin_ap, blk * 512, xnT[:, :, blk * 512:(blk + 1) * 512], psT, lnb)
        for h in range(16):
            b_lw = [lw[0].v, lw[1].v]
            b_mmq, b_evq, b_acd = mmq.v, evq.v, acd.v
            b_peT, b_evT, b_dvd = peT.v, evT.v, dvd.v
            with nc.Block() as b:
                @b.sync
                def _(sync):
                    for w_i in range(3):
                        pr = w_i % 2
                        if w_i >= 2:
                            sync.wait_ge(mmq.h, b_mmq + 4)
                        _load_w_slab(sync, p, p.d["wqkvb"], h * 384 + w_i * 128, 128, wsl[pr], lw[pr].h)
                @b.tensor
                def _(tn):
                    for w_i in range(3):
                        pr = w_i % 2
                        tn.wait_ge(lw[pr].h, b_lw[pr] + 16 * (w_i // 2 + 1))
                        for nchunk in range(4):
                            ph = nchunk % 2
                            if w_i + nchunk > 0:
                                tn.wait_ge(acd.h, b_acd + w_i * 4 + nchunk)
                            for c in range(16):
                                mi = tn.matmul(psQ[ph][:], wsl[pr][:, c, :],
                                               xnT[:, c, nchunk * 512:(nchunk + 1) * 512],
                                               start=(c == 0), stop=(c == 15))
                            mi.then_inc(mmq.h, 1)
                    for src_i, src in enumerate((kT_sb, vT_sb)):
                        tn.wait_ge(acd.h, b_acd + 8 + src_i * 4)
                        for tc in range(16):
                            k = src_i * 16 + tc
                            if k >= 2:
                                tn.wait_ge(evT.h, b_evT + k - 1)
                            tn.transpose(psT[k % 2][:], src[:, tc * 128:(tc + 1) * 128],
                                         p.s["ident"][:]).then_inc(peT.h, 1)
                @b.scalar
                def _(s):
                    for w_i, dst in enumerate((qT_sb, kT_sb, vT_sb)):
                        for nchunk in range(4):
                            ph = nchunk % 2
                            s.wait_ge(mmq.h, b_mmq + w_i * 4 + nchunk + 1)
                            s.activation(out=dst[:, nchunk * 512:(nchunk + 1) * 512],
                                         in_=psQ[ph][:], func=AF.Silu).then_inc(acd.h, 1)
                    s.wait_ge(dvd.h, b_dvd + 1)
                    s.activation(out=qdec_bf[:], in_=arr_sb[:], func=AF.Exp,
                                 scale=dcol[:, 0:1]).then_inc(acd.h, 1)
                    s.activation(out=dtmp[:], in_=idxc_sb[:], func=AF.Exp, scale=dcol[:, 0:1])
                    s.activation(out=dcol[:, 4:6], in_=dcol[:, 12:14], func=AF.Exp,
                                 scale=dcol[:, 0:1])
                    mi = s.activation(out=dcol[:, 8:9], in_=cc(p, C_256), func=AF.Exp,
                                      scale=dcol[:, 0:1])
                    mi.then_inc(acd.h, 1)
                @b.vector
                def _(v):
                    v.tensor_scalar(dcol[:, 0:1], slope_sb[:, h:h + 1], cc(p, C_NEG1), None, ALU.mult)
                    v.memset(kv[:], 0.0)
                    mi = v.memset(kv_bf[:], 0.0)
                    mi.then_inc(dvd.h, 1)
                    v.wait_ge(acd.h, b_acd + 14)
                    v.tensor_mul(ddT[:, 0, :], dtmp[:, 0, :], maskd_sb[:, 0, :])
                    mi = v.tensor_mul(ddT[:, 1, :], dtmp[:, 1, :], maskd_sb[:, 1, :])
                    mi.then_inc(dvd.h, 1)
                    for src_i in range(2):
                        for tc in range(16):
                            k = src_i * 16 + tc
                            v.wait_ge(peT.h, b_peT + k + 1)
                            if src_i == 0:
                                v.tensor_scalar(kd_tm[:, tc, :], psT[k % 2][:],
                                                dcol[:, 4 + tc % 2:5 + tc % 2], None, ALU.mult).then_inc(evT.h, 1)
                            else:
                                v.tensor_copy(v_tm[:, tc, :], psT[k % 2][:]).then_inc(evT.h, 1)
            lw[0].v += 32; lw[1].v += 16
            mmq.v += 12
            acd.v += 14
            dvd.v += 2
            peT.v += 32
            evT.v += 32
            b_mms, b_dvd = mms.v, dvd.v
            with nc.Block() as b:
                @b.tensor
                def _(tn):
                    for i in range(8):
                        if i >= 1:
                            tn.wait_ge(dvd.h, b_dvd + 5 * (i - 1) + 3)
                        for ncj in range(2):
                            mi = tn.matmul(psQ[ncj][:, 0:256], kT_sb[:, i * 256 + ncj * 128:i * 256 + (ncj + 1) * 128],
                                           qT_sb[:, i * 256:(i + 1) * 256], start=True, stop=True)
                        mi.then_inc(mms.h, 1)
                        tn.wait_ge(dvd.h, b_dvd + 5 * i + 3)
                        for mc in range(2):
                            first = True
                            if i >= 1:
                                tn.matmul(psO2[mc][:], qd_bf[:, mc * 128:(mc + 1) * 128], kv_bf[:],
                                          start=True, stop=False)
                                first = False
                            for ncj in range(2):
                                mi = tn.matmul(psO2[mc][:], qkd_bf[:, ncj, mc * 128:(mc + 1) * 128],
                                               v_tm[:, i * 2 + ncj, :],
                                               start=first and ncj == 0, stop=(ncj == 1))
                        mi.then_inc(mms.h, 1)
                        for ncj in range(2):
                            mi = tn.matmul(psD[:], kd_tm[:, i * 2 + ncj, :], v_tm[:, i * 2 + ncj, :],
                                           start=(ncj == 0), stop=(ncj == 1))
                        mi.then_inc(mms.h, 1)
                @b.vector
                def _(v):
                    gap = lambda: v.tensor_copy(lnb["gapt"][0][:], lnb["gapt"][1][:])
                    for i in range(8):
                        tc0, tc1 = i * 2, i * 2 + 1
                        v.wait_ge(mms.h, b_mms + 3 * i + 1)
                        v.tensor_mul(qkd_bf[:, 0, :], psQ[0][:, 0:256], ddT[:, 0, :])
                        v.tensor_mul(qkd_bf[:, 1, :], psQ[1][:, 0:256], ddT[:, 1, :])
                        mi = v.tensor_mul(qd_bf[:], qT_sb[:].rearrange("p (b m) -> p b m", b=8)[:, i, :], qdec_bf[:])
                        mi.then_inc(dvd.h, 3)
                        v.wait_ge(mms.h, b_mms + 3 * i + 2)
                        v.tensor_copy(o_buf[:, tc0, h, :], psO2[0][:])
                        v.tensor_copy(o_buf[:, tc1, h, :], psO2[1][:])
                        v.tensor_mul(sqo[:, 0, :], o_buf[:, tc0, h, :], o_buf[:, tc0, h, :])
                        v.tensor_mul(sqo[:, 1, :], o_buf[:, tc1, h, :], o_buf[:, tc1, h, :])
                        v.reduce_sum(ssqcols[:, tc0, h:h + 1], sqo[:, 0, :], axis=AX)
                        v.reduce_sum(ssqcols[:, tc1, h:h + 1], sqo[:, 1, :], axis=AX)
                        v.wait_ge(mms.h, b_mms + 3 * i + 3)
                        v.scalar_tensor_tensor(kv[:], kv[:], dcol[:, 8:9], psD[:], ALU.mult, ALU.add)
                        gap()
                        mi = v.tensor_copy(kv_bf[:], kv[:])
                        mi.then_inc(dvd.h, 2)
            mms.v += 24
            dvd.v += 40
        b_dvd, b_acd = dvd.v, acd.v
        with nc.Block() as b:
            @b.vector
            def _(v):
                v.reduce_sum(dcol[:, 0:16].rearrange("p (c o) -> p c o", o=1), ssqcols[:], axis=AX)
                v.tensor_copy(lnb["gapt"][0][:], lnb["gapt"][1][:])
                mi = v.tensor_scalar(slope_sb[:, 0:16], dcol[:, 0:16], cc(p, C_INV2048), None, ALU.mult)
                mi.then_inc(dvd.h, 1)
                v.wait_ge(acd.h, b_acd + 1)
                mi = v.reciprocal(rstd16[:], sq16[:])
                mi.then_inc(dvd.h, 1)
            @b.scalar
            def _(s):
                s.wait_ge(dvd.h, b_dvd + 1)
                mi = s.activation(out=sq16[:], in_=slope_sb[:, 0:16], func=AF.Sqrt, bias=cc(p, C_EPSRMS))
                mi.then_inc(acd.h, 1)
        dvd.v += 2
        acd.v += 1
        b_lw = [lw[0].v, lw[1].v]
        b_mmq, b_acd, b_dvd, b_peT, b_evT = mmq.v, acd.v, dvd.v, peT.v, evT.v
        b_stg = [stg[0].v, stg[1].v]
        with nc.Block() as b:
            @b.sync
            def _(sync):
                for fc in range(4):
                    if fc >= 1:
                        sync.wait_ge(mmq.h, b_mmq + fc * 16)
                    _load_w_slab(sync, p, p.d["wg0b"], fc * 512, 512, wgl, lw[0].h)
            @b.tensor
            def _(tn):
                nT2 = 0
                for fc in range(4):
                    tn.wait_ge(lw[0].h, b_lw[0] + 16 * (fc + 1))
                    for tile in range(16):
                        j = fc * 16 + tile
                        ph = j % 2
                        if j >= 2:
                            tn.wait_ge(acd.h, b_acd + j - 1)
                        for c in range(16):
                            mi = tn.matmul(psQ[ph][:], xnT[:, c, tile * 128:(tile + 1) * 128],
                                           wgl[:, c, :], start=(c == 0), stop=(c == 15))
                        mi.then_inc(mmq.h, 1)
                        tn.wait_ge(dvd.h, b_dvd + 2 * j + 2)
                        for tpc in range(4):
                            k = nT2
                            if k >= 2:
                                tn.wait_ge(evT.h, b_evT + k - 1)
                            tn.transpose(psT[k % 2][:], gt_bf[:, tpc * 128:(tpc + 1) * 128],
                                         p.s["ident"][:]).then_inc(peT.h, 1)
                            nT2 += 1
            @b.scalar
            def _(s):
                for fc in range(4):
                    for tile in range(16):
                        j = fc * 16 + tile
                        ph = j % 2
                        s.wait_ge(mmq.h, b_mmq + j + 1)
                        if j >= 1:
                            s.wait_ge(dvd.h, b_dvd + 2 * j)
                        s.activation(out=gsig[:], in_=psQ[ph][:], func=AF.Sigmoid).then_inc(acd.h, 1)
                        s.wait_ge(evT.h, b_evT + 4 * (j + 1))
                        dst = bass.AP(p.d["gsp"].tensor, (fc * 4) * 128 * 16 * 128 + tile * 128,
                                      [[16 * 128, 128], [128 * 16 * 128, 4], [1, 128]])
                        s.dma_start(out=dst, in_=gT_sb[:]).then_inc(stg[j % 2].h, 16)
                s.wait_ge(stg[0].h, b_stg[0] + 16 * 32)
                s.wait_ge(stg[1].h, b_stg[1] + 16 * 32)
            @b.vector
            def _(v):
                for fc in range(4):
                    for tile in range(16):
                        j = fc * 16 + tile
                        v.wait_ge(acd.h, b_acd + j + 1)
                        v.tensor_scalar(gt_bf[:], gsig[:], rstd16[:, tile:tile + 1], None, ALU.mult)
                        v.tensor_copy(lnb["gapt"][0][:], lnb["gapt"][1][:])
                        ov = o_buf[:, tile, fc * 4:(fc + 1) * 4, :].rearrange("p a b -> p (a b)")
                        mi = v.tensor_mul(gt_bf[:], gt_bf[:], ov)
                        mi.then_inc(dvd.h, 2)
                        for tpc in range(4):
                            k = j * 4 + tpc
                            v.wait_ge(peT.h, b_peT + k + 1)
                            if j >= 1 and tpc == 0:
                                v.wait_ge(stg[(j - 1) % 2].h, b_stg[(j - 1) % 2] + 16 * ((j - 1) // 2 + 1))
                            v.tensor_copy(gT_sb[:, tpc, :], psT[k % 2][:]).then_inc(evT.h, 1)
        lw[0].v += 64
        mmq.v += 64
        acd.v += 64
        dvd.v += 128
        peT.v += 256
        evT.v += 256
        stg[0].v += 16 * 32
        stg[1].v += 16 * 32
        es2.close()
        psY = [p.psum(es, "psY", [128, 512], F32) for _ in range(4)]
        for blk in range(4):
            b_lgt = lgt.v
            with nc.Block() as b:
                @b.sync
                def _(sync):
                    for oc in range(16):
                        src = bass.AP(p.d["gsp"].tensor, oc * 128 * 16 * 128 + blk * 4 * 128,
                                      [[16 * 128, 128], [128, 4], [1, 128]])
                        sync.dma_start(out=gT_buf[:, oc, :, :], in_=src).then_inc(lgt.h, 16)
                    sync.wait_ge(lgt.h, b_lgt + 16 * 16)
            lgt.v += 16 * 16
            lhsT_list = [[gT_buf[:, oc, tc, :] for tc in range(4)] for oc in range(16)]
            emit_down_generic(p, es, p.d["wo0b"], xin_ap, xout_ap, blk * 512, psY,
                              lhsT_list, C_ONE, dnb)



N_CORES = 8
CAST_JOBS = [("w1t", "w1b", D, H), ("w3t", "w3b", D, H), ("wqkvt", "wqkvb", D, 3 * D),
             ("wg0t", "wg0b", D, D), ("wq1", "wq1b", D, D), ("wk1", "wk1b", D, D),
             ("wv1", "wv1b", D, D), ("wg1", "wg1b", D, H), ("wu1", "wu1b", D, H),
             ("wd1", "wd1b", H, D), ("wo1", "wo1b", D, D)]
SCRATCH_ALL = ["xa", "xb", "w2p", "utmp", "w1b", "w3b", "wqkvb", "wg0b", "wo0b",
               "wq1b", "wk1b", "wv1b", "wo1b", "wg1b", "wu1b", "wd1b", "qtsp", "gsp"]


def build_program():
    p = Prog()
    declare_io(p)
    declare_scratch(p, SCRATCH_ALL)
    emit_consts(p)
    emit_cast_weights(p, CAST_JOBS)
    emit_fold_norm_wo0(p)
    emit_w2prep(p)
    emit_ffn(p, "l0", p.d["xin"], p.d["xa"])
    emit_lightning(p, p.d["xa"], p.d["xb"])
    emit_ffn(p, "l0", p.d["xb"], p.d["xa"])
    emit_ffn(p, "l1", p.d["xa"], p.d["xb"])
    emit_attn_l1(p, p.d["xb"], p.d["xa"])
    emit_ffn(p, "l1", p.d["xa"], p.d["yout"])
    p.es.close()
    return p


def kernel(**inputs):
    from concourse.bass_utils import run_bass_kernel_spmd
    shared = host_prep(inputs, 0)
    del shared["xin"]
    in_maps = []
    for c in range(N_CORES):
        m = dict(shared)
        m["xin"] = np.ascontiguousarray(inputs["L0_attn_hidden_states"][c]).astype(np.float32, copy=False)
        in_maps.append(m)
    p = build_program()
    res = run_bass_kernel_spmd(p.nc, in_maps, core_ids=list(range(N_CORES)))
    out = np.stack([res.results[c]["yout"] for c in range(N_CORES)], axis=0)
    return out
